# version 01 (final)
# baseline (speedup 1.0000x reference)
import numpy as np
import ml_dtypes
from contextlib import ExitStack

import concourse.bass as bass
import concourse.mybir as mybir
from concourse.bass_utils import run_bass_kernel_spmd
from concourse.alu_op_type import AluOpType as ALU

AF = mybir.ActivationFunctionType
F32 = mybir.dt.float32
BF16 = mybir.dt.bfloat16
AXX = mybir.AxisListType.X
BFNP = ml_dtypes.bfloat16

D = 1024
S = 2048
BPC = 2
NTOK = BPC * S
H = 16
DK = 64
I0 = 2816
I1 = 4096
EPS = 1e-5
EXP_OFF = -12.0

ENGINES = ("sync", "gpsimd", "scalar", "vector", "tensor")
ATTN_HEADS = H
ATTN_STRIPES = 4
ATTN_BATCHES = BPC
ATTN_DO_LORA = True
ATTN_DO_V = True
ATTN_DO_QKV = True


def host_consts():
    eye_bf = np.eye(128, dtype=BFNP)
    eye_f32 = np.eye(128, dtype=np.float32)
    rope_dim = DK // 2
    inv_freq = 1.0 / (6000.0 ** (np.arange(0, rope_dim, 2, dtype=np.float32) / rope_dim))
    t = np.arange(S, dtype=np.float32)
    freqs = np.outer(t, inv_freq).astype(np.float32)
    emb = np.concatenate([freqs, freqs], axis=-1)
    cos_t = np.ascontiguousarray(np.cos(emb).T.astype(np.float32))
    sin = np.sin(emb).astype(np.float32)
    sins = np.ascontiguousarray(
        np.concatenate([-sin[:, :16], sin[:, 16:]], axis=-1).T)
    masks = np.zeros((4, 128, 512), dtype=np.float32)
    for j in range(4):
        kg = 128 * j + np.arange(128)[:, None]
        qg = np.arange(512)[None, :]
        masks[j] = np.where(kg > qg, -1e30, EXP_OFF)
    return dict(c_eyeb=eye_bf, c_eyef=eye_f32, c_cos=cos_t, c_sins=sins,
                c_mask=masks.reshape(4 * 128, 512))


class Prog:
    def __init__(self, name):
        self.nc = bass.Bass("TRN2", target_bir_lowering=False, name=name,
                            detect_race_conditions=False)
        self.es = ExitStack()
        self.cnt = {}
        self.sem = {}
        self.stages = []
        self.cur = None

    def dram(self, name, shape, dt=F32, kind="ExternalInput"):
        return self.nc.dram_tensor(name, shape, dt, kind=kind).ap()

    def sb(self, name, shape, dt=F32):
        return self.es.enter_context(self.nc.sbuf_tensor(name, shape, dt))

    def ps(self, name, shape, dt=F32):
        return self.es.enter_context(self.nc.psum_tensor(name, shape, dt))

    def mksem(self, *names):
        for n in names:
            self.sem[n] = self.es.enter_context(self.nc.semaphore(n))
            self.cnt[n] = 0

    def stage(self):
        self.cur = {e: [] for e in ENGINES}
        self.stages.append(self.cur)

    def op(self, eng, fn, sem=None, n=1, wait=None):
        val = None
        if sem is not None:
            self.cnt[sem] += n
            val = self.cnt[sem]
        s = self.sem
        wait = [w for w in (wait or []) if w[1] > 0]

        def run(e, fn=fn, sem=sem, n=n, wait=wait, val=val):
            for (wn, wv) in wait:
                e.wait_ge(s[wn], wv)
            inst = fn(e)
            if sem is not None:
                inst.then_inc(s[sem], n)
        self.cur[eng].append(run)
        return val

    def v(self, sem):
        return self.cnt[sem]

    def cons(self, *sems):
        return [(sn, self.cnt[sn]) for sn in sems]

    def emit(self):
        for st in self.stages:
            with self.nc.Block(no_gpsimd_drain=True) as block:
                for eng in ENGINES:
                    if st[eng]:
                        getattr(block, eng)(
                            lambda e, fns=tuple(st[eng]): [f(e) for f in fns])
        self.es.close()
        return self.nc


def emit_ln(p, x_ap, y_ap, sq, lnt, wait_dma=None):
    w = list(wait_dma or [])
    p.op("scalar", lambda e: e.activation(sq[:, 0:D], x_ap, AF.Square),
         sem="a", wait=w + p.cons("v"))
    aq = p.v("a")
    rv1 = p.op("vector", lambda e: e.tensor_reduce(lnt[:, 0:1], x_ap, AXX, ALU.add),
               wait=w, sem="v")
    rv2 = p.op("vector", lambda e: e.tensor_reduce(lnt[:, 1:2], sq[:, 0:D], AXX, ALU.add),
               wait=[("a", aq)], sem="v")
    nm = p.op("vector", lambda e: e.tensor_scalar(lnt[:, 2:3], lnt[:, 0:1], -1.0 / D,
                                                  None, ALU.mult),
              wait=[("v", rv1)], sem="v")
    m2 = p.op("vector", lambda e: e.tensor_scalar(lnt[:, 3:4], lnt[:, 2:3], lnt[:, 2:3],
                                                  EPS, ALU.mult, ALU.subtract),
              wait=[("v", nm)], sem="v")
    p.op("vector", lambda e: e.tensor_scalar(lnt[:, 4:5], lnt[:, 1:2], 1.0 / D,
                                             lnt[:, 3:4], ALU.mult, ALU.subtract),
         sem="v", wait=[("v", rv2), ("v", m2)])
    vv = p.v("v")
    p.op("scalar", lambda e: e.activation(lnt[:, 5:6], lnt[:, 4:5], AF.Ln),
         wait=[("v", vv)])
    p.op("scalar", lambda e: e.activation(lnt[:, 6:7], lnt[:, 5:6], AF.Exp, scale=-0.5),
         sem="a")
    av = p.v("a")
    nr = p.op("vector", lambda e: e.tensor_scalar(lnt[:, 7:8], lnt[:, 2:3], lnt[:, 6:7],
                                                  None, ALU.mult),
              wait=[("a", av)], sem="v")
    p.op("vector", lambda e: e.tensor_scalar(y_ap, x_ap, lnt[:, 6:7], lnt[:, 7:8],
                                             ALU.mult, ALU.add),
         sem="v", wait=p.cons("p") + [("v", nr)])
    return p.v("v")


def bcast16(ap2d, nb):
    return bass.AP(ap2d.tensor, ap2d.offset, [ap2d.ap[0], [1, nb], [0, 16]])


def view3d(ap, nb):
    return bass.AP(ap.tensor, ap.offset, [ap.ap[0], [16, nb], [1, 16]])


QMIDS = [(0.25, 0.5), (0.75, 0.5), (1.25, 0.5), (1.75, 0.5), (2.5, 1.0), (3.5, 1.0),
         (5.0, 2.0)]


def emit_quant(p, x_ap, out_ap, scr, F, wait_v=None):
    nb = F // 16
    w = list(wait_v or [])
    rbm = p.op("vector", lambda e: e.tensor_reduce(
        scr["bm"][:, 0:nb], view3d(x_ap, nb), AXX, ALU.max,
        apply_absolute_value=True), wait=w, sem="v")
    b6 = p.op("vector", lambda e: e.tensor_scalar(scr["bm6"][:, 0:nb], scr["bm"][:, 0:nb],
                                                  1.0 / 6.0, 1e-12, ALU.mult, ALU.max),
              wait=[("v", rbm)], sem="v")
    rrc = p.op("vector", lambda e: e.reciprocal(scr["rcp"][:, 0:nb],
                                                scr["bm6"][:, 0:nb]),
               sem="v", wait=[("v", b6)])
    aab = p.op("scalar", lambda e: e.activation(scr["tmp"][:, 0:F], x_ap, AF.Abs),
               sem="a", wait=w + p.cons("v"))
    p.op("vector", lambda e: e.tensor_tensor(
        view3d(scr["yn"][:, 0:F], nb), view3d(scr["tmp"][:, 0:F], nb),
        bcast16(scr["rcp"][:, 0:nb], nb), ALU.mult),
        wait=[("v", rrc), ("a", aab)])
    yn = scr["yn"][:, 0:F]
    m0, w0 = QMIDS[0]
    p.op("vector", lambda e: e.tensor_scalar(scr["acc"][:, 0:F], yn, m0, w0, ALU.is_gt,
                                             ALU.mult))
    for (m, wt) in QMIDS[1:]:
        p.op("vector", lambda e, m=m, wt=wt: e.tensor_scalar(
            scr["tmp"][:, 0:F], yn, m, wt, ALU.is_gt, ALU.mult))
        p.op("vector", lambda e: e.tensor_tensor(scr["acc"][:, 0:F], scr["acc"][:, 0:F],
                                                 scr["tmp"][:, 0:F], ALU.add))
    p.op("scalar", lambda e: e.activation(scr["sg"][:, 0:F], x_ap, AF.Sign),
         sem="a", wait=w + p.cons("v"))
    av = p.v("a")
    p.op("vector", lambda e: e.tensor_tensor(scr["acc"][:, 0:F], scr["acc"][:, 0:F],
                                             scr["sg"][:, 0:F], ALU.mult),
         wait=[("a", av)])
    p.op("vector", lambda e: e.tensor_tensor(
        view3d(out_ap, nb), view3d(scr["acc"][:, 0:F], nb),
        bcast16(scr["bm6"][:, 0:nb], nb), ALU.mult), sem="v",
        wait=p.cons("p") + [("v", b6)])
    return p.v("v")


def mkqscr(p, tag=""):
    return dict(bm=p.sb("q_bm" + tag, [128, 32]), bm6=p.sb("q_bm6" + tag, [128, 32]),
                rcp=p.sb("q_rcp" + tag, [128, 32]), yn=p.sb("q_yn" + tag, [128, 512]),
                acc=p.sb("q_acc" + tag, [128, 512]), tmp=p.sb("q_tmp" + tag, [128, 512]),
                sg=p.sb("q_sg" + tag, [128, 512], BF16))


def emit_tr_group(p, src_aps, p_tr, dst_ap, eye, ew=128, first_wait=None):
    n = len(src_aps)
    fw = list(first_wait or []) + p.cons("a")
    for j, sap in enumerate(src_aps):
        p.op("tensor", lambda e, j=j, sap=sap: e.transpose(
            p_tr[:, j * ew:(j + 1) * ew], sap, eye),
            sem=("p" if j == n - 1 else None), wait=(fw if j == 0 else None))
    pv = p.v("p")
    p.op("scalar", lambda e: e.copy(dst_ap, p_tr[:, 0:n * ew]), sem="a",
         wait=[("p", pv)])
    return p.v("a")


def strided_dst(buf, offset_blocks, stride_blocks, nblk, ew=128):
    ap = buf[:]
    return bass.AP(ap.tensor, ap.offset + offset_blocks * ew,
                   [ap.ap[0], [stride_blocks * ew, nblk], [1, ew]])


def build_ffn1(debug=False):
    p = Prog("ffn1" + ("dbg" if debug else ""))
    p.dbg_extra = {}
    x_d = p.dram("x", [NTOK, D])
    w1_d = p.dram("L1_ffn_w1", [I1, D])
    b1_d = p.dram("L1_ffn_b1", [I1])
    w2_d = p.dram("L1_ffn_w2", [D, I1])
    b2_d = p.dram("L1_ffn_b2", [D])
    eyeb_d = p.dram("c_eyeb", [128, 128], BF16)
    eyef_d = p.dram("c_eyef", [128, 128])
    y_d = p.dram("y", [NTOK, D], kind="ExternalOutput")

    eyeb = p.sb("eyeb", [128, 128], BF16)
    eyef = p.sb("eyef", [128, 128])
    ones = p.sb("ones", [1, 128], BF16)
    wst = p.sb("wst", [128, 2048])
    sg = p.sb("sg", [128, 1024], BF16)
    alph = p.sb("alph", [128, 8])
    bw1T = p.sb("bw1T", [128, 8 * I1], BF16)
    bw2T = p.sb("bw2T", [128, 32 * D], BF16)
    b1T = p.sb("b1T", [128, 32])
    b1st = p.sb("b1st", [32, 128])
    b2rb = p.sb("b2rb", [1, D], BF16)
    x_t = [p.sb(f"x_t{i}", [128, D]) for i in range(2)]
    sq = p.sb("sqs", [128, D])
    lnt = p.sb("lnt", [128, 8])
    ybf = [p.sb(f"ybf{i}", [128, D], BF16) for i in range(2)]
    yT = p.sb("yT", [128, 8 * 512], BF16)
    h_fm = p.sb("h_fm", [128, 32 * 512], BF16)
    o_t = [p.sb(f"o_t{i}", [128, D]) for i in range(2)]

    p_h = [p.ps(f"p_h{i}", [128, 512]) for i in range(2)]
    p_tr = p.ps("p_tr", [128, 512], BF16)
    p_o = [p.ps(f"p_o{i}", [128, 512]) for i in range(2)]

    p.mksem("d", "st", "v", "a", "p", "g")

    p.stage()
    p.op("sync", lambda e: e.dma_start(eyeb[:], eyeb_d[:]), sem="d", n=16)
    p.op("sync", lambda e: e.dma_start(eyef[:], eyef_d[:]), sem="d", n=16)
    p.op("sync", lambda e: e.dma_start(
        b1st[:], bass.AP(b1_d.tensor, 0, [[128, 32], [1, 128]])), sem="d", n=16)
    p.op("gpsimd", lambda e: e.dma_start(
        b2rb[:], bass.AP(b2_d.tensor, 0, [[D, 1], [1, D]])), sem="g", n=16)
    p.op("sync", lambda e: e.nop(), wait=p.cons("g"))
    dv = p.v("d")
    p.op("vector", lambda e: e.memset(ones[:], 1.0))
    p.op("tensor", lambda e: e.transpose(p_o[0][:, 0:32], b1st[:], eyef[0:32, 0:32]),
         sem="p", wait=[("d", dv)])
    pv = p.v("p")
    p.op("scalar", lambda e: e.copy(b1T[:], p_o[0][:, 0:32]), wait=[("p", pv)])

    for r in range(32):
        p.stage()
        p.op("sync", lambda e, r=r: e.dma_start(wst[:, 0:D],
                                                w1_d[r * 128:(r + 1) * 128, :]),
             sem="d", n=16)
        dv = p.v("d")
        p.op("scalar", lambda e: e.activation(sq[:, 0:D], wst[:, 0:D], AF.Abs),
             sem="a", wait=[("d", dv)] + p.cons("v"))
        aabs = p.v("a")
        rva = p.op("vector", lambda e: e.tensor_reduce(alph[:, 0:1], sq[:, 0:D], AXX,
                                                       ALU.add),
                   wait=[("a", aabs)], sem="v")
        alv = p.op("vector", lambda e: e.tensor_scalar(alph[:, 1:2], alph[:, 0:1],
                                                       1.0 / D, None, ALU.mult),
                   wait=[("v", rva)], sem="v")
        p.op("scalar", lambda e: e.activation(sg[:, 0:D], wst[:, 0:D], AF.Sign),
             sem="a", wait=[("d", dv)] + p.cons("v"))
        av = p.v("a")
        p.op("vector", lambda e: e.tensor_scalar(ybf[0][:, 0:D], sg[:, 0:D],
                                                 alph[:, 1:2], None, ALU.mult),
             sem="v", wait=[("a", av), ("v", alv)] + p.cons("p"))
        vv = p.v("v")
        for half in range(2):
            srcs = [ybf[0][:, (half * 4 + j) * 128:(half * 4 + j + 1) * 128]
                    for j in range(4)]
            emit_tr_group(p, srcs, p_tr,
                          strided_dst(bw1T, half * 4 * 32 + r, 32, 4), eyeb[:],
                          first_wait=[("v", vv)])

    for r2 in range(8):
        p.stage()
        for q in range(2):
            p.op("sync", lambda e, r2=r2, q=q: e.dma_start(
                wst[:, 0:2048],
                w2_d[r2 * 128:(r2 + 1) * 128, q * 2048:(q + 1) * 2048]),
                sem="d", n=16, wait=(p.cons("v") if q == 1 else None))
            dv = p.v("d")
            for hh in range(2):
                p.op("scalar", lambda e, hh=hh: e.activation(
                    sq[:, 0:D], wst[:, hh * D:(hh + 1) * D], AF.Abs), sem="a",
                    wait=[("d", dv)] + p.cons("v"))
                aabs = p.v("a")
                p.op("vector", lambda e, q=q, hh=hh: e.tensor_reduce(
                    alph[:, 2 * q + hh:2 * q + hh + 1], sq[:, 0:D], AXX, ALU.add),
                    sem="v", wait=[("a", aabs)])
        r6 = p.op("vector", lambda e: e.tensor_reduce(alph[:, 6:7], alph[:, 0:4], AXX,
                                                      ALU.add),
                  wait=p.cons("v"), sem="v")
        al7 = p.op("vector", lambda e: e.tensor_scalar(alph[:, 7:8], alph[:, 6:7],
                                                       1.0 / I1, None, ALU.mult),
                   sem="v", wait=[("v", r6)])
        for q in range(2):
            p.stage()
            p.op("sync", lambda e, r2=r2, q=q: e.dma_start(
                wst[:, 0:2048],
                w2_d[r2 * 128:(r2 + 1) * 128, q * 2048:(q + 1) * 2048]), sem="d", n=16)
            dv = p.v("d")
            for sub in range(2):
                p.op("scalar", lambda e, sub=sub: e.activation(
                    sg[:, 0:1024], wst[:, sub * 1024:(sub + 1) * 1024], AF.Sign),
                    sem="a", wait=[("d", dv)] + p.cons("v"))
                av = p.v("a")
                p.op("vector", lambda e: e.tensor_scalar(ybf[0][:, 0:D], sg[:, 0:1024],
                                                         alph[:, 7:8], None, ALU.mult),
                     sem="v", wait=[("a", av), ("v", al7)] + p.cons("p"))
                vv = p.v("v")
                if debug and r2 == 0 and q == 0 and sub == 0:
                    alsnap = p.sb("alsnap", [128, 8])
                    ysnap = p.sb("ysnap", [128, 64])
                    p.op("gpsimd", lambda e: e.dma_start(alsnap[:], alph[:]),
                         sem="d", n=16, wait=[("v", vv)]) if False else None
                    p.op("vector", lambda e: e.tensor_copy(alsnap[:], alph[:]),
                         wait=[("v", vv)])
                    p.op("vector", lambda e: e.tensor_copy(ysnap[:], ybf[0][:, 0:64]),
                         sem="v")
                    p.dbg_extra = {"alsnap": alsnap, "ysnap": ysnap}
                for grp in range(2):
                    srcs = [ybf[0][:, (grp * 4 + j) * 128:(grp * 4 + j + 1) * 128]
                            for j in range(4)]
                    c32_0 = q * 16 + sub * 8 + grp * 4
                    emit_tr_group(p, srcs, p_tr,
                                  strided_dst(bw2T, c32_0 * 8 + r2, 8, 4),
                                  eyeb[:], first_wait=[("v", vv)])

    for st8 in range(8):
        p.stage()
        trv = []
        for t in range(4):
            ti = st8 * 4 + t
            p.op("sync", lambda e, ti=ti, t=t: e.dma_start(
                x_t[t % 2][:], x_d[ti * 128:(ti + 1) * 128, :]), sem="d", n=16,
                wait=p.cons("v"))
            dv = p.v("d")
            yv = emit_ln(p, x_t[t % 2][:], ybf[t % 2][:], sq, lnt, wait_dma=[("d", dv)])
            for half in range(2):
                srcs = [ybf[t % 2][:, (half * 4 + j) * 128:(half * 4 + j + 1) * 128]
                        for j in range(4)]
                emit_tr_group(p, srcs, p_tr,
                              strided_dst(yT, half * 4 * 4 + t, 4, 4), eyeb[:],
                              first_wait=[("v", yv)])

        p.stage()
        evacs = []
        for j in range(32):
            mmw = None
            if j >= 2:
                mmw = [("a", evacs[j - 2])]
            for kc in range(8):
                p.op("tensor", lambda e, j=j, kc=kc: e.matmul(
                    p_h[j % 2][:], bw1T[:, (kc * 32 + j) * 128:(kc * 32 + j + 1) * 128],
                    yT[:, kc * 512:(kc + 1) * 512], start=(kc == 0), stop=(kc == 7)),
                    sem=("p" if kc == 7 else None),
                    wait=(mmw if kc == 0 else None))
            pv = p.v("p")
            p.op("scalar", lambda e, j=j: e.activation(
                h_fm[:, j * 512:(j + 1) * 512], p_h[j % 2][:], AF.Gelu,
                bias=b1T[:, j:j + 1]), sem="a", wait=[("p", pv)])
            evacs.append(p.v("a"))

        p.stage()
        for t in range(4):
            ti = st8 * 4 + t
            p.op("sync", lambda e, ti=ti, t=t: e.dma_start(
                x_t[t % 2][:], x_d[ti * 128:(ti + 1) * 128, :]),
                sem="d", n=16, wait=p.cons("v"))
            dv = p.v("d")
            for g in range(2):
                for j in range(32):
                    p.op("tensor", lambda e, t=t, g=g, j=j: e.matmul(
                        p_o[g][:], h_fm[:, j * 512 + t * 128:j * 512 + (t + 1) * 128],
                        bw2T[:, (j * 8 + g * 4) * 128:(j * 8 + g * 4 + 4) * 128],
                        start=(j == 0), stop=False),
                        wait=(p.cons("v") if (j == 0 and g == 0) else None))
                p.op("tensor", lambda e, g=g: e.matmul(
                    p_o[g][:], ones[0:1, :], b2rb[0:1, g * 512:(g + 1) * 512],
                    start=False, stop=True), sem="p")
            pv = p.v("p")
            p.op("vector", lambda e, t=t: e.scalar_tensor_tensor(
                o_t[t % 2][:, 0:512], p_o[0][:], 0.5, x_t[t % 2][:, 0:512],
                ALU.mult, ALU.add),
                wait=[("p", pv), ("d", dv)] + p.cons("st"))
            p.op("vector", lambda e, t=t: e.scalar_tensor_tensor(
                o_t[t % 2][:, 512:1024], p_o[1][:], 0.5, x_t[t % 2][:, 512:1024],
                ALU.mult, ALU.add), sem="v")
            vv = p.v("v")
            p.op("sync", lambda e, ti=ti, t=t: e.dma_start(
                y_d[ti * 128:(ti + 1) * 128, :], o_t[t % 2][:]),
                sem="st", n=16, wait=[("v", vv)])

    if debug:
        dbg = {"yT": yT, "h_fm": h_fm, "bw1T": bw1T, "bw2T": bw2T, "b1T": b1T,
               "b2rb": b2rb}
        dbg.update(p.dbg_extra)
        p.stage()
        for nm, buf in dbg.items():
            shp = [buf[:].ap[0][1], buf[:].ap[1][1]]
            od = p.dram("dbg_" + nm, shp, kind="ExternalOutput")
            p.op("gpsimd", lambda e, od=od, buf=buf: e.dma_start(od, buf[:]),
                 sem="g", n=16, wait=p.cons("v", "a", "p"))
    p.stage()
    p.op("sync", lambda e: e.nop(), wait=p.cons("st") + p.cons("g"))
    return p.emit()


def build_ffn0():
    p = Prog("ffn0")
    x_d = p.dram("x", [NTOK, D])
    wg_d = p.dram("L0_ffn_w_gate", [I0, D])
    wu_d = p.dram("L0_ffn_w_up", [I0, D])
    wd_d = p.dram("L0_ffn_w_down", [D, I0])
    eyeb_d = p.dram("c_eyeb", [128, 128], BF16)
    y_d = p.dram("y", [NTOK, D], kind="ExternalOutput")

    eyeb = p.sb("eyeb", [128, 128], BF16)
    ones = p.sb("ones", [1, 128], BF16)
    wst = p.sb("wst", [128, I0])
    qscr = mkqscr(p)
    qout = p.sb("qout", [128, 512], BF16)
    wgT = p.sb("wgT", [128, 8 * I0], BF16)
    wuT = p.sb("wuT", [128, 8 * I0], BF16)
    wdT = p.sb("wdT", [128, 22 * D], BF16)
    x_t = [p.sb(f"x_t{i}", [128, D]) for i in range(2)]
    sq = p.sb("sqs", [128, D])
    lnt = p.sb("lnt", [128, 8])
    yf = p.sb("yf", [128, D])
    ybf = p.sb("ybf", [128, D], BF16)
    yT = p.sb("yT", [128, 8 * 512], BF16)
    hsil = p.sb("hsil", [128, 512])
    hf = p.sb("hf", [128, 512])
    hq = p.sb("hq", [128, 512], BF16)
    hqT = p.sb("hqT", [128, 22 * 128], BF16)
    o_t = [p.sb(f"o_t{i}", [128, D]) for i in range(2)]

    p_g = p.ps("p_g", [128, 512])
    p_u = p.ps("p_u", [128, 512])
    p_tr = p.ps("p_tr", [128, 512], BF16)
    p_o = [p.ps(f"p_o{i}", [128, 512]) for i in range(2)]

    p.mksem("d", "st", "v", "a", "p", "g")

    p.stage()
    p.op("sync", lambda e: e.dma_start(eyeb[:], eyeb_d[:]), sem="d", n=16)
    p.op("vector", lambda e: e.memset(ones[:], 1.0))

    for (w_d, wT) in ((wg_d, wgT), (wu_d, wuT)):
        for r in range(22):
            p.stage()
            p.op("sync", lambda e, w_d=w_d, r=r: e.dma_start(
                wst[:, 0:D], w_d[r * 128:(r + 1) * 128, :]), sem="d", n=16)
            dv = p.v("d")
            for sl in range(2):
                qv = emit_quant(p, wst[:, sl * 512:(sl + 1) * 512], qout[:, 0:512],
                                qscr, 512, wait_v=[("d", dv)])
                srcs = [qout[:, j * 128:(j + 1) * 128] for j in range(4)]
                emit_tr_group(p, srcs, p_tr,
                              strided_dst(wT, (sl * 4) * 22 + r, 22, 4), eyeb[:],
                              first_wait=[("v", qv)])

    for r8 in range(8):
        p.stage()
        for q in range(3):
            w = 1024 if q < 2 else 768
            p.op("sync", lambda e, r8=r8, q=q, w=w: e.dma_start(
                wst[:, q * 1024:q * 1024 + w],
                wd_d[r8 * 128:(r8 + 1) * 128, q * 1024:q * 1024 + w]), sem="d", n=16)
        dv = p.v("d")
        for sl in range(6):
            w = 512 if sl < 5 else 256
            nb = w // 128
            qv = emit_quant(p, wst[:, sl * 512:sl * 512 + w], qout[:, 0:w],
                            qscr, w, wait_v=[("d", dv)])
            srcs = [qout[:, j * 128:(j + 1) * 128] for j in range(nb)]
            emit_tr_group(p, srcs, p_tr, strided_dst(wdT, (sl * 4) * 8 + r8, 8, nb),
                          eyeb[:], first_wait=[("v", qv)])

    for st8 in range(8):
        for t in range(4):
            ti = st8 * 4 + t
            p.stage()
            p.op("sync", lambda e, ti=ti, t=t: e.dma_start(
                x_t[t % 2][:], x_d[ti * 128:(ti + 1) * 128, :]), sem="d", n=16)
            dv = p.v("d")
            yv = emit_ln(p, x_t[t % 2][:], yf[:, 0:D], sq, lnt, wait_dma=[("d", dv)])
            for sl in range(2):
                qv = emit_quant(p, yf[:, sl * 512:(sl + 1) * 512],
                                ybf[:, sl * 512:(sl + 1) * 512], qscr, 512,
                                wait_v=[("v", yv)])
                srcs = [ybf[:, (sl * 4 + j) * 128:(sl * 4 + j + 1) * 128]
                        for j in range(4)]
                emit_tr_group(p, srcs, p_tr, strided_dst(yT, sl * 4 * 4 + t, 4, 4),
                              eyeb[:], first_wait=[("v", qv)])

        for t in range(4):
            p.stage()
            ngs = [(0, 512), (512, 512), (1024, 512), (1536, 512), (2048, 512),
                   (2560, 256)]
            for (n0, nw) in ngs:
                r0 = n0 // 128
                nbk = nw // 128
                for kc in range(8):
                    p.op("tensor", lambda e, t=t, kc=kc, r0=r0, nw=nw: e.matmul(
                        p_g[:, 0:nw],
                        yT[:, kc * 512 + t * 128:kc * 512 + (t + 1) * 128],
                        wgT[:, (kc * 22 + r0) * 128:(kc * 22 + r0) * 128 + nw],
                        start=(kc == 0), stop=(kc == 7)),
                        sem=("p" if kc == 7 else None),
                        wait=(p.cons("a", "v") if kc == 0 else None))
                for kc in range(8):
                    p.op("tensor", lambda e, t=t, kc=kc, r0=r0, nw=nw: e.matmul(
                        p_u[:, 0:nw],
                        yT[:, kc * 512 + t * 128:kc * 512 + (t + 1) * 128],
                        wuT[:, (kc * 22 + r0) * 128:(kc * 22 + r0) * 128 + nw],
                        start=(kc == 0), stop=(kc == 7)),
                        sem=("p" if kc == 7 else None))
                pv = p.v("p")
                p.op("scalar", lambda e, nw=nw: e.activation(
                    hsil[:, 0:nw], p_g[:, 0:nw], AF.Silu), sem="a",
                    wait=[("p", pv)] + p.cons("v"))
                av = p.v("a")
                p.op("vector", lambda e, nw=nw: e.tensor_tensor(
                    hf[:, 0:nw], hsil[:, 0:nw], p_u[:, 0:nw], ALU.mult), sem="v",
                    wait=[("a", av)])
                hv = p.v("v")
                qv = emit_quant(p, hf[:, 0:nw], hq[:, 0:nw], qscr, nw,
                                wait_v=[("v", hv)])
                srcs = [hq[:, j * 128:(j + 1) * 128] for j in range(nbk)]
                emit_tr_group(p, srcs, p_tr, strided_dst(hqT, r0, 1, nbk), eyeb[:],
                              first_wait=[("v", qv)])
            avh = p.v("a")
            ti = st8 * 4 + t
            p.op("sync", lambda e, ti=ti, t=t: e.dma_start(
                x_t[t % 2][:], x_d[ti * 128:(ti + 1) * 128, :]),
                sem="d", n=16, wait=p.cons("v"))
            dv = p.v("d")
            for g in range(2):
                for c22 in range(22):
                    p.op("tensor", lambda e, g=g, c22=c22: e.matmul(
                        p_o[g][:], hqT[:, c22 * 128:(c22 + 1) * 128],
                        wdT[:, (c22 * 8 + g * 4) * 128:(c22 * 8 + g * 4 + 4) * 128],
                        start=(c22 == 0), stop=(c22 == 21)),
                        sem=("p" if c22 == 21 else None),
                        wait=([("a", avh)] + p.cons("v") if (c22 == 0 and g == 0)
                              else None))
            pv = p.v("p")
            p.op("vector", lambda e, t=t: e.scalar_tensor_tensor(
                o_t[t % 2][:, 0:512], p_o[0][:], 0.5, x_t[t % 2][:, 0:512],
                ALU.mult, ALU.add), wait=[("p", pv), ("d", dv)] + p.cons("st"))
            p.op("vector", lambda e, t=t: e.scalar_tensor_tensor(
                o_t[t % 2][:, 512:1024], p_o[1][:], 0.5, x_t[t % 2][:, 512:1024],
                ALU.mult, ALU.add), sem="v")
            vv = p.v("v")
            p.op("sync", lambda e, ti=ti, t=t: e.dma_start(
                y_d[ti * 128:(ti + 1) * 128, :], o_t[t % 2][:]),
                sem="st", n=16, wait=[("v", vv)])

    p.stage()
    p.op("sync", lambda e: e.nop(), wait=p.cons("st"))
    return p.emit()


def build_attn(layer, debug=False):
    p = Prog(f"attn{layer}" + ("dbg" if debug else ""))
    x_d = p.dram("x", [NTOK, D])
    eyeb_d = p.dram("c_eyeb", [128, 128], BF16)
    eyef_d = p.dram("c_eyef", [128, 128])
    y_d = p.dram("y", [NTOK, D], kind="ExternalOutput")
    if layer == 0:
        wqkv_d = p.dram("L0_attn_W_qkv", [3, H, DK, D])
        wo_d = p.dram("L0_attn_W_o", [D, D])
        cos_d = p.dram("c_cos", [32, S])
        sins_d = p.dram("c_sins", [32, S])
        mask_d = p.dram("c_mask", [4 * 128, 512])
    else:
        wqkv_d = p.dram("L1_attn_w_qkv", [D, 3 * D])
        bqkv_d = p.dram("L1_attn_b_qkv", [3 * D])
        aq_d = p.dram("L1_attn_a_q", [DK, 16])
        bq_d = p.dram("L1_attn_b_q", [16, DK])
        av_d = p.dram("L1_attn_a_v", [DK, 16])
        bv_d = p.dram("L1_attn_b_v", [16, DK])
        wo_d = p.dram("L1_attn_w_proj", [D, D])
        bp_d = p.dram("L1_attn_b_proj", [D])

    eyeb = p.sb("eyeb", [128, 128], BF16)
    eyef = p.sb("eyef", [128, 128])
    ones = p.sb("ones", [1, 128], BF16)
    x_t = [p.sb(f"x_t{i}", [128, D]) for i in range(2)]
    sq = p.sb("sqs", [128, D])
    lnt = p.sb("lnt", [128, 8])
    ybf = p.sb("ybf", [128, D], BF16)
    yT = p.sb("yT", [128, 8 * S], BF16)
    wsl = p.sb("wsl", [128, 24 * 64], BF16)
    q_sb = p.sb("q_sb", [64, S], BF16)
    k_sb = p.sb("k_sb", [64, S], BF16)
    v_aug = p.sb("v_aug", [128, 16 * 65], BF16)
    pt_sb = p.sb("pt_sb", [128, 16 * 512], BF16)
    att = p.sb("att", [128, 8 * S], BF16)
    woT = p.sb("woT", [128, 8 * D], BF16)
    o_t = [p.sb(f"o_t{i}", [128, D]) for i in range(2)]
    rs_sb = p.sb("rs_sb", [1, 512])
    oav = p.sb("oav", [64, 512])
    noff = p.sb("noff", [128, 1])
    onesf = p.sb("onesf", [1, 64])
    if layer == 0:
        cosb = p.sb("cosb", [32, S])
        sinsb = p.sb("sinsb", [32, S])
        maskb = p.sb("maskb", [128, 4 * 512])
        qf = p.sb("qf", [64, S])
        qsw = p.sb("qsw", [32, S])
        qsw2 = p.sb("qsw2", [32, S])
        qcs = p.sb("qcs", [32, S])
        stm = p.sb("stm", [128, 512])
        wst0 = p.sb("wst0", [64, D], BF16)
        won = att
    else:
        v_sb = p.sb("v_sb", [64, S], BF16)
        bqT = p.sb("bqT", [128, 24])
        bqst = p.sb("bqst", [24, 128])
        bprb = p.sb("bprb", [1, D], BF16)
        lor = p.sb("lor", [64, 160], BF16)
        lst = p.sb("lst", [64, 64])
        t_sb = p.sb("t_sb", [16, S], BF16)

    p_qkv = [p.ps(f"p_qkv{i}", [64, 512]) for i in range(2)]
    p_st = [p.ps(f"p_st{i}", [128, 512]) for i in range(2)]
    p_av = p.ps("p_av", [65, 512])
    p_tr = p.ps("p_tr", [128, 512], BF16)
    p_o = [p.ps(f"p_o{i}", [128, 512]) for i in range(2)]

    p.mksem("d", "st", "v", "a", "p", "g")

    p.stage()
    p.op("sync", lambda e: e.dma_start(eyeb[:], eyeb_d[:]), sem="d", n=16)
    p.op("sync", lambda e: e.dma_start(eyef[:], eyef_d[:]), sem="d", n=16)
    p.op("vector", lambda e: e.memset(ones[:], 1.0))
    p.op("vector", lambda e: e.memset(onesf[:], 1.0))
    p.op("vector", lambda e: e.memset(noff[:], EXP_OFF))
    p.op("vector", lambda e: e.memset(
        bass.AP(v_aug[:].tensor, v_aug[:].offset + 64, [v_aug[:].ap[0], [65, 16], [1, 1]]),
        1.0))
    if layer == 0:
        p.op("sync", lambda e: e.dma_start(cosb[:], cos_d[:]), sem="d", n=16)
        p.op("sync", lambda e: e.dma_start(sinsb[:], sins_d[:]), sem="d", n=16)
        for j in range(4):
            p.op("sync", lambda e, j=j: e.dma_start(
                maskb[:, j * 512:(j + 1) * 512], mask_d[j * 128:(j + 1) * 128, :]),
                sem="d", n=16)
        p.op("gpsimd", lambda e: e.dma_start(
            bass.AP(won[:].tensor, won[:].offset, [won[:].ap[0], [D, 8], [1, D]]),
            wo_d[:].rearrange("(c p) n -> p c n", p=128)), sem="g", n=16)
        gv = p.v("g")
        dv = p.v("d")
        for dr in range(8):
            for grp in range(2):
                srcs = [won[:, dr * D + (grp * 4 + j) * 128:dr * D + (grp * 4 + j + 1) * 128]
                        for j in range(4)]
                emit_tr_group(p, srcs, p_tr, strided_dst(woT, grp * 4 * 8 + dr, 8, 4),
                              eyeb[:], first_wait=[("d", dv), ("g", gv)])
    else:
        p.op("sync", lambda e: e.dma_start(
            bqst[:], bass.AP(bqkv_d.tensor, 0, [[128, 24], [1, 128]])), sem="d", n=16)
        p.op("gpsimd", lambda e: e.dma_start(
            bprb[:], bass.AP(bp_d.tensor, 0, [[D, 1], [1, D]])), sem="g", n=16)
        p.op("gpsimd", lambda e: e.dma_start(lor[:, 0:16], aq_d[:]), sem="g", n=16)
        p.op("gpsimd", lambda e: e.dma_start(lor[:, 16:32], av_d[:]), sem="g", n=16)
        p.op("gpsimd", lambda e: e.dma_start(lst[0:16, 0:64], bq_d[:]), sem="g", n=16)
        p.op("gpsimd", lambda e: e.dma_start(lst[32:48, 0:64], bv_d[:]), sem="g", n=16)
        p.op("gpsimd", lambda e: e.dma_start(
            bass.AP(woT[:].tensor, woT[:].offset, [woT[:].ap[0], [D, 8], [1, D]]),
            wo_d[:].rearrange("(c p) n -> p c n", p=128)), sem="g", n=16)
        gv = p.v("g")
        dv = p.v("d")
        p.op("tensor", lambda e: e.transpose(p_o[0][:, 0:24], bqst[:],
                                             eyef[0:24, 0:24]), sem="p",
             wait=[("d", dv)])
        pv = p.v("p")
        p.op("scalar", lambda e: e.copy(bqT[:], p_o[0][:, 0:24]), sem="a",
             wait=[("p", pv)])
        p.op("vector", lambda e: e.tensor_scalar(lor[0:16, 32:96], lst[0:16, 0:64], 2.0,
                                                 None, ALU.mult), wait=[("g", gv)])
        p.op("vector", lambda e: e.tensor_scalar(lor[0:16, 96:160], lst[32:48, 0:64],
                                                 2.0, None, ALU.mult), sem="v")

    for b in (range(ATTN_BATCHES) if not globals().get('ATTN_REV') else reversed(range(ATTN_BATCHES))):
        base = b * (S // 128)
        for t4 in range(4):
            p.stage()
            for t in range(4):
                ti = t4 * 4 + t
                p.op("sync", lambda e, ti=ti, bi=base + ti: e.dma_start(
                    x_t[ti % 2][:], x_d[bi * 128:(bi + 1) * 128, :]),
                    sem="d", n=16, wait=p.cons("v"))
                dv = p.v("d")
                yv = emit_ln(p, x_t[ti % 2][:], ybf[:], sq, lnt, wait_dma=[("d", dv)])
                for half in range(2):
                    srcs = [ybf[:, (half * 4 + j) * 128:(half * 4 + j + 1) * 128]
                            for j in range(4)]
                    emit_tr_group(p, srcs, p_tr,
                                  strided_dst(yT, half * 4 * 16 + ti, 16, 4), eyeb[:],
                                  first_wait=[("v", yv)])

        for h in range(ATTN_HEADS):
            p.stage()
            if layer == 0:
                for i3 in range(3):
                    p.op("gpsimd", lambda e, i3=i3, h=h: e.dma_start(
                        wst0[:], bass.AP(wqkv_d.tensor, (i3 * H + h) * DK * D,
                                         [[D, DK], [1, D]])),
                        sem="g", n=16, wait=p.cons("a"))
                    gv = p.v("g")
                    srcs = [wst0[:, c * 128:(c + 1) * 128] for c in range(8)]
                    emit_tr_group(p, srcs, p_tr, wsl[:, i3 * 512:(i3 + 1) * 512],
                                  eyeb[0:64, 0:64], ew=64, first_wait=[("g", gv)])
                wv = p.v("a")
            else:
                for i3 in range(3):
                    for c in range(8):
                        p.op("gpsimd", lambda e, i3=i3, c=c, h=h: e.dma_start(
                            wsl[:, (i3 * 8 + c) * 64:(i3 * 8 + c + 1) * 64],
                            wqkv_d[c * 128:(c + 1) * 128,
                                   i3 * D + h * 64:i3 * D + h * 64 + 64]),
                            sem="g", n=16,
                            wait=(p.cons("p") if (i3 == 0 and c == 0) else None))
                dv = p.v("g")
                wv = None

            def qkv_fm(i3, dst_sb, bias_col):
                evs = []
                for qq in range(4):
                    for kc in range(8):
                        if layer == 0:
                            lhs = wsl[:, i3 * 512 + kc * 64:i3 * 512 + (kc + 1) * 64]
                        else:
                            lhs = wsl[:, (i3 * 8 + kc) * 64:(i3 * 8 + kc + 1) * 64]
                        wt = None
                        if kc == 0:
                            wt = ([("a", wv)] if layer == 0 else [("g", dv)])
                            if qq == 0:
                                wt = wt + p.cons("v")
                            if qq >= 2:
                                wt = wt + [("a", evs[qq - 2])]
                        p.op("tensor", lambda e, lhs=lhs, kc=kc, qq=qq: e.matmul(
                            p_qkv[qq % 2][:], lhs,
                            yT[:, kc * S + qq * 512:kc * S + (qq + 1) * 512],
                            start=(kc == 0), stop=(kc == 7)),
                            sem=("p" if kc == 7 else None), wait=wt)
                    pv = p.v("p")
                    if layer == 1:
                        p.op("scalar", lambda e, qq=qq, dst_sb=dst_sb, bias_col=bias_col:
                             e.activation(dst_sb[:, qq * 512:(qq + 1) * 512],
                                          p_qkv[qq % 2][:], AF.Identity, bias=bias_col),
                             sem="a", wait=[("p", pv)] + (p.cons("v") if qq < 2 else []))
                    else:
                        p.op("scalar", lambda e, qq=qq, dst_sb=dst_sb: e.copy(
                            dst_sb[:, qq * 512:(qq + 1) * 512], p_qkv[qq % 2][:]),
                            sem="a", wait=[("p", pv)] + (p.cons("v") if qq < 2 else []))
                    evs.append(p.v("a"))
                return p.v("a")

            if layer == 0:
                for (i3, dst) in ((0, q_sb), (1, k_sb)):
                    av = qkv_fm(i3, qf, None)
                    p.op("gpsimd", lambda e: e.dma_start(qsw[0:16, :], qf[16:32, :]),
                         sem="g", n=16, wait=[("a", av)] + p.cons("v"))
                    p.op("gpsimd", lambda e: e.dma_start(qsw[16:32, :], qf[0:16, :]),
                         sem="g", n=16)
                    dvs = p.v("g")
                    p.op("vector", lambda e: e.tensor_tensor(
                        qsw2[0:32, :], qsw[0:32, :], sinsb[:], ALU.mult),
                        wait=[("g", dvs)])
                    p.op("vector", lambda e: e.tensor_tensor(
                        qcs[0:32, :], qf[0:32, :], cosb[:], ALU.mult),
                        wait=[("a", av)])
                    p.op("vector", lambda e, dst=dst: e.tensor_tensor(
                        dst[0:32, :], qcs[0:32, :], qsw2[0:32, :], ALU.add))
                    p.op("vector", lambda e, dst=dst: e.tensor_copy(
                        dst[32:64, :], qf[32:64, :]), sem="v")
                vevs = []
                for kt in range(16):
                    for kc in range(8):
                        wt = None
                        if kc == 0:
                            wt = [("a", wv)]
                            if kt >= 2:
                                wt = wt + [("a", vevs[kt - 2])]
                        p.op("tensor", lambda e, kt=kt, kc=kc: e.matmul(
                            p_o[kt % 2][0:128, 0:64],
                            yT[:, kc * S + kt * 128:kc * S + (kt + 1) * 128],
                            wsl[:, 2 * 512 + kc * 64:2 * 512 + (kc + 1) * 64],
                            start=(kc == 0), stop=(kc == 7)),
                            sem=("p" if kc == 7 else None), wait=wt)
                    pv = p.v("p")
                    p.op("scalar", lambda e, kt=kt: e.copy(
                        v_aug[:, kt * 65:kt * 65 + 64], p_o[kt % 2][0:128, 0:64]),
                        sem="a", wait=[("p", pv)])
                    vevs.append(p.v("a"))
            else:
                qbias = bqT[(h % 2) * 64:(h % 2) * 64 + 64, h // 2:h // 2 + 1]
                kcol = H + h
                kbias = bqT[(kcol % 2) * 64:(kcol % 2) * 64 + 64, kcol // 2:kcol // 2 + 1]
                vcol = 2 * H + h
                vbias = bqT[(vcol % 2) * 64:(vcol % 2) * 64 + 64, vcol // 2:vcol // 2 + 1]

                def lora(src_sb, a_sl, b_sl):
                    av0 = p.v("a")
                    tevs = []
                    for qq in range(4):
                        wt = [("a", av0)] if qq == 0 else None
                        if qq >= 2:
                            wt = [("a", tevs[qq - 2])]
                        p.op("tensor", lambda e, qq=qq, a_sl=a_sl, src_sb=src_sb:
                             e.matmul(p_st[qq % 2][0:16, :], a_sl,
                                      src_sb[:, qq * 512:(qq + 1) * 512],
                                      start=True, stop=True),
                             sem="p", wait=wt)
                        pv = p.v("p")
                        p.op("scalar", lambda e, qq=qq: e.copy(
                            t_sb[:, qq * 512:(qq + 1) * 512], p_st[qq % 2][0:16, :]),
                            sem="a", wait=[("p", pv)])
                        tevs.append(p.v("a"))
                    avt = p.v("a")
                    uevs = []
                    for qq in range(4):
                        wt = [("a", avt)] if qq == 0 else None
                        if qq >= 2:
                            wt = [("v", uevs[qq - 2])]
                        p.op("tensor", lambda e, qq=qq, b_sl=b_sl: e.matmul(
                            p_qkv[qq % 2][:], b_sl, t_sb[:, qq * 512:(qq + 1) * 512],
                            start=True, stop=True), sem="p", wait=wt)
                        pv = p.v("p")
                        p.op("vector", lambda e, qq=qq, src_sb=src_sb: e.tensor_tensor(
                            src_sb[:, qq * 512:(qq + 1) * 512],
                            src_sb[:, qq * 512:(qq + 1) * 512],
                            p_qkv[qq % 2][:], ALU.add), sem="v", wait=[("p", pv)])
                        uevs.append(p.v("v"))

                if ATTN_DO_QKV:
                    qkv_fm(0, q_sb, qbias)
                    if ATTN_DO_LORA:
                        lora(q_sb, lor[:, 0:16], lor[0:16, 32:96])
                    qkv_fm(1, k_sb, kbias)
                    qkv_fm(2, v_sb, vbias)
                    if ATTN_DO_LORA:
                        lora(v_sb, lor[:, 16:32], lor[0:16, 96:160])
                vv = p.v("v")
                for grp in (range(2) if ATTN_DO_V else []):
                    srcs = [v_sb[:, (grp * 8 + j) * 128:(grp * 8 + j + 1) * 128]
                            for j in range(8)]
                    dst = bass.AP(v_aug[:].tensor, v_aug[:].offset + grp * 8 * 65,
                                  [v_aug[:].ap[0], [65, 8], [1, 64]])
                    emit_tr_group(p, srcs, p_tr, dst, eyeb[0:64, 0:64], ew=64,
                                  first_wait=[("v", vv)])

            if debug and b == 0 and h == 1 and layer == 1:
                atth_d = p.dram("dbg_atth", [64, 2048], kind="ExternalOutput")
                p.op("gpsimd", lambda e, od=atth_d: e.dma_start(
                    od, att[0:64, 0:2048]), sem="g", n=16, wait=p.cons("v"))
            for s4 in range(ATTN_STRIPES):
                ktmax = (s4 + 1) * 4 if layer == 0 else 16
                expa = []
                for kt in range(ktmax):
                    wt = p.cons("a", "v") if kt == 0 else None
                    if kt >= 2:
                        wt = [("a", expa[kt - 2])]
                    p.op("tensor", lambda e, kt=kt, s4=s4: e.matmul(
                        p_st[kt % 2][:], k_sb[:, kt * 128:(kt + 1) * 128],
                        q_sb[:, s4 * 512:(s4 + 1) * 512], start=True, stop=True),
                        sem="p", wait=wt)
                    pv = p.v("p")
                    diag = (layer == 0) and (kt >= s4 * 4)
                    if diag:
                        j = kt - s4 * 4
                        p.op("vector", lambda e, kt=kt, j=j: e.scalar_tensor_tensor(
                            stm[:], p_st[kt % 2][:], 0.125,
                            maskb[:, j * 512:(j + 1) * 512], ALU.mult, ALU.add),
                            sem="v", wait=[("p", pv)] + p.cons("a"))
                        vv = p.v("v")
                        p.op("scalar", lambda e, kt=kt: e.activation(
                            pt_sb[:, kt * 512:(kt + 1) * 512], stm[:], AF.Exp),
                            sem="a", wait=[("v", vv)])
                    else:
                        p.op("scalar", lambda e, kt=kt: e.activation(
                            pt_sb[:, kt * 512:(kt + 1) * 512], p_st[kt % 2][:], AF.Exp,
                            bias=noff[:, 0:1], scale=0.125), sem="a", wait=[("p", pv)])
                    expa.append(p.v("a"))
                ava = p.v("a")
                for kt in range(ktmax):
                    p.op("tensor", lambda e, kt=kt, ktmax=ktmax: e.matmul(
                        p_av[:], v_aug[:, kt * 65:(kt + 1) * 65],
                        pt_sb[:, kt * 512:(kt + 1) * 512],
                        start=(kt == 0), stop=(kt == ktmax - 1)),
                        sem=("p" if kt == ktmax - 1 else None),
                        wait=([("a", ava)] + p.cons("v") if kt == 0 else None))
                pv = p.v("p")
                p.op("vector", lambda e: e.reciprocal(rs_sb[:], p_av[64:65, :]),
                     sem="v", wait=[("p", pv)])
                vv = p.v("v")
                p.op("scalar", lambda e: e.copy(oav[:], p_av[0:64, :]), sem="a",
                     wait=[("p", pv)])
                ava2 = p.v("a")
                p.op("tensor", lambda e: e.matmul(
                    p_o[0][0:64, 0:512], onesf[0:1, 0:64], rs_sb[0:1, :],
                    start=True, stop=True), sem="p", wait=[("v", vv), ("a", ava2)])
                pv = p.v("p")
                p.op("vector", lambda e, h=h, s4=s4: e.tensor_tensor(
                    att[(h % 2) * 64:(h % 2) * 64 + 64,
                        (h // 2) * S + s4 * 512:(h // 2) * S + (s4 + 1) * 512],
                    oav[:], p_o[0][0:64, 0:512], ALU.mult), sem="v", wait=[("p", pv)])

        if debug and b == 0:
            att0_d = p.dram("dbg_att0", [128, 8 * S], kind="ExternalOutput")
            p.stage()
            p.op("gpsimd", lambda e, od=att0_d: e.dma_start(od, att[:]),
                 sem="g", n=16, wait=p.cons("v", "a", "p"))
        p.stage()
        for ti in range(16):
            p.op("sync", lambda e, ti=ti, bi=base + ti: e.dma_start(
                x_t[ti % 2][:], x_d[bi * 128:(bi + 1) * 128, :]),
                sem="d", n=16, wait=(p.cons("v") if ti >= 2 else None))
            dv = p.v("d")
            for g in range(2):
                for c in range(8):
                    if layer == 0:
                        rhs = woT[:, (c * 8 + g * 4) * 128:(c * 8 + g * 4 + 4) * 128]
                    else:
                        rhs = woT[:, c * D + g * 512:c * D + (g + 1) * 512]
                    p.op("tensor", lambda e, rhs=rhs, c=c, g=g, ti=ti: e.matmul(
                        p_o[g][:], att[:, c * S + ti * 128:c * S + (ti + 1) * 128],
                        rhs, start=(c == 0), stop=(layer == 0 and c == 7)),
                        wait=(p.cons("v") if (g == 0 and c == 0) else None),
                        sem=("p" if (layer == 0 and c == 7) else None))
                if layer == 1:
                    p.op("tensor", lambda e, g=g: e.matmul(
                        p_o[g][:], ones[0:1, :], bprb[0:1, g * 512:(g + 1) * 512],
                        start=False, stop=True), sem="p")
            pv = p.v("p")
            p.op("vector", lambda e, ti=ti: e.tensor_tensor(
                o_t[ti % 2][:, 0:512], x_t[ti % 2][:, 0:512], p_o[0][:], ALU.add),
                wait=[("p", pv), ("d", dv)] + p.cons("st"))
            p.op("vector", lambda e, ti=ti: e.tensor_tensor(
                o_t[ti % 2][:, 512:1024], x_t[ti % 2][:, 512:1024], p_o[1][:], ALU.add),
                sem="v")
            vv = p.v("v")
            p.op("sync", lambda e, ti=ti, bi=base + ti: e.dma_start(
                y_d[bi * 128:(bi + 1) * 128, :], o_t[ti % 2][:]),
                sem="st", n=16, wait=[("v", vv)])

    if debug:
        dbg = {"yT": yT, "q_sb": q_sb, "k_sb": k_sb, "v_aug": v_aug, "pt_sb": pt_sb,
               "att": att, "woT": woT, "wsl": wsl, "oav": oav, "rs_sb": rs_sb}
        p.stage()
        for nm, buf in dbg.items():
            shp = [buf[:].ap[0][1], buf[:].ap[1][1]]
            od = p.dram("dbg_" + nm, shp, kind="ExternalOutput")
            p.op("gpsimd", lambda e, od=od, buf=buf: e.dma_start(od, buf[:]),
                 sem="g", n=16, wait=p.cons("v", "a", "p", "st"))
    p.stage()
    p.op("sync", lambda e: e.nop(), wait=p.cons("st") + p.cons("g"))
    return p.emit()


_CACHE = {}


def _progs():
    if "p" not in _CACHE:
        _CACHE["p"] = dict(ffn0=build_ffn0(), attn0=build_attn(0),
                           ffn1=build_ffn1(), attn1=build_attn(1))
        _CACHE["c"] = host_consts()
    return _CACHE["p"], _CACHE["c"]


SEQS = [
    ("ffn0", ["L0_ffn_w_gate", "L0_ffn_w_up", "L0_ffn_w_down"]),
    ("attn0", ["L0_attn_W_qkv", "L0_attn_W_o"]),
    ("ffn0", ["L0_ffn_w_gate", "L0_ffn_w_up", "L0_ffn_w_down"]),
    ("ffn1", ["L1_ffn_w1", "L1_ffn_b1", "L1_ffn_w2", "L1_ffn_b2"]),
    ("attn1", ["L1_attn_w_qkv", "L1_attn_b_qkv", "L1_attn_a_q", "L1_attn_b_q",
               "L1_attn_a_v", "L1_attn_b_v", "L1_attn_w_proj", "L1_attn_b_proj"]),
    ("ffn1", ["L1_ffn_w1", "L1_ffn_b1", "L1_ffn_w2", "L1_ffn_b2"]),
]


def kernel(**inputs):
    inp = {k: np.ascontiguousarray(np.asarray(v)) for k, v in inputs.items()}
    progs, consts = _progs()
    state = np.ascontiguousarray(inp["L0_attn_x"].reshape(8, NTOK, D))
    for name, wkeys in SEQS:
        base = {k: inp[k] for k in wkeys}
        base["c_eyeb"] = consts["c_eyeb"]
        base["c_eyef"] = consts["c_eyef"]
        if name == "attn0":
            base["c_cos"] = consts["c_cos"]
            base["c_sins"] = consts["c_sins"]
            base["c_mask"] = consts["c_mask"]
        in_maps = [dict(base, x=state[i]) for i in range(8)]
        res = run_bass_kernel_spmd(progs[name], in_maps, core_ids=list(range(8)))
        state = np.stack([np.asarray(r["y"]) for r in res.results])
    return np.ascontiguousarray(state.reshape(16, S, D)).astype(np.float32)
